# version 01
# baseline (speedup 1.0000x reference)
import numpy as np
import ml_dtypes
from contextlib import ExitStack
from types import SimpleNamespace

import concourse.bass as bass
import concourse.mybir as mybir
from concourse.bass_utils import run_bass_kernel_spmd

F32 = mybir.dt.float32
BF16 = mybir.dt.bfloat16
AF = mybir.ActivationFunctionType
ALU = mybir.AluOpType
AX = mybir.AxisListType
BFNP = ml_dtypes.bfloat16

P = 128
D = 2048
HID = 8192
NH = 16
NCH_D = D // P
T_OWN = 1024
T_PRE = 1024
TW = T_OWN + T_PRE
C = 64
NCHUNK = TW // C
NPRE = T_PRE // C
HG = 2
NGRP = NH // HG
QSCALE = float(P) ** -0.5
NTILE = TW // P
NTILE_OWN = T_OWN // P
OWN0 = NTILE - NTILE_OWN
HC_ALL = HID // P
E8 = 8
NE8 = HC_ALL // E8
GW = HG * P

ENGINES = ("tensor", "scalar", "vector", "gpsimd", "sync")
BITS = [128.0, 64.0, 32.0, 16.0, 8.0, 4.0, 2.0, 1.0]


class Sem:
    def __init__(self, es, nc, name):
        self.h = es.enter_context(nc.semaphore(name))
        self.n = 0


class Prog:
    def __init__(self, nc, es):
        self.nc = nc
        self.es = es
        self.order = {e: [] for e in ENGINES}
        self.prod = {e: Sem(es, nc, f"prod_{e}") for e in ENGINES}
        self._dsems = {}
        self.tail_clean = {e: False for e in ENGINES}

    def dma_sem(self, name):
        if name not in self._dsems:
            self._dsems[name] = Sem(self.es, self.nc, name)
        return self._dsems[name]

    def add(self, eng, build, wait=(), sig=False):
        w = [(s.h, v) for (s, v) in wait if s is not None]
        incs = []
        tok = None
        if sig:
            sp = self.prod[eng]
            sp.n += 1
            incs.append((sp.h, 1))
            tok = (sp, sp.n)
        self.order[eng].append([w, build, incs, "dp"])
        self.tail_clean[eng] = sig
        return tok

    def dma(self, eng, out, in_, sem, wait=(), transpose=False,
            allow_slow=False):
        w = [(s.h, v) for (s, v) in wait if s is not None]
        sem.n += 16
        tok = (sem, sem.n)
        if transpose:
            def build(h, o=out, i=in_):
                return h.dma_start_transpose(o, i)
        elif allow_slow:
            def build(h, o=out, i=in_):
                return h.dma_start(out=o, in_=i,
                                   allow_slow_non_contiguous=True)
        else:
            def build(h, o=out, i=in_):
                return h.dma_start(out=o, in_=i)
        self.order[eng].append([w, build, [(sem.h, 16)], "dma"])
        return tok

    def tok(self, eng):
        sp = self.prod[eng]
        return (sp, sp.n)

    def wait_only(self, eng, wait):
        w = [(s.h, v) for (s, v) in wait if s is not None]
        self.order[eng].append([w, lambda h: h.nop(), [], "seq"])

    def _sig_tail(self, eng):
        sp = self.prod[eng]
        if self.tail_clean[eng]:
            return (sp, sp.n)
        for item in reversed(self.order[eng]):
            if item[3] == "dp":
                sp.n += 1
                item[2].append((sp.h, 1))
                self.tail_clean[eng] = True
                return (sp, sp.n)
        sp.n += 1
        self.order[eng].append([[], lambda h: h.nop(), [(sp.h, 1)], "seq"])
        self.tail_clean[eng] = True
        return (sp, sp.n)

    def barrier(self, dma_full=()):
        sems = list(dma_full)
        for s in self._dsems.values():
            if s not in sems:
                sems.append(s)
        for s in sems:
            self.wait_only("scalar", [(s, s.n)])
        toks = [self._sig_tail(e) for e in ENGINES]
        for e in ENGINES:
            self.wait_only(e, [t for t in toks if t[0] is not self.prod[e]])

    def finalize(self, block):
        for e in ENGINES:
            items = self.order[e]

            def emit(h, items=items):
                for (w, build, incs, _kind) in items:
                    for (sh, v) in w:
                        h.wait_ge(sh, v)
                    ins = build(h)
                    for (sh, k) in incs:
                        ins.then_inc(sh, k)

            getattr(block, e)(emit)


def _emit_p0(cx):
    nc, pr = cx.nc, cx.pr
    with ExitStack() as ps:
        def sb(name, shape, dt):
            return ps.enter_context(nc.sbuf_tensor(name, shape, dt))
        xts = [sb("xt0", [P, D], F32), sb("xt1", [P, D], F32)]
        sqs = sb("sqs", [P, D], F32)
        xnts = [sb("xnt0", [P, D], BF16), sb("xnt1", [P, D], BF16)]
        ypt = sb("ypt", [P, D], F32)
        zzt = sb("zzt", [P, D], F32)
        mmt = sb("mmt", [P, D], F32)
        xqts = [sb("xqt0", [P, D], BF16), sb("xqt1", [P, D], BF16)]
        scr2 = sb("scr2", [P, 12 * NTILE], F32)

        ldx = [pr.dma_sem("ldx0"), pr.dma_sem("ldx1")]
        tpx = [pr.dma_sem("tpx0"), pr.dma_sem("tpx1")]
        tpq = [pr.dma_sem("tpq0"), pr.dma_sem("tpq1")]
        free_x = [None, None]
        free_xn = [None, None]
        free_xq = [None, None]
        t_rd2 = None
        t_xq_prev = None
        t_scr_free = [None, None]

        for j in range(NTILE):
            sl = j % 2
            own = j >= OWN0
            scr = scr2[:, 12 * j:12 * (j + 1)]
            xt, xn_t, xq_t = xts[sl], xnts[sl], xqts[sl]
            src = (cx.x_pre[j * P:(j + 1) * P, :] if j < OWN0
                   else cx.x_own[(j - OWN0) * P:(j - OWN0 + 1) * P, :])
            t_ld = pr.dma("sync", xt[:], src, ldx[sl],
                          wait=free_x[sl] or [])

            tq = pr.add("scalar", lambda h, xt=xt: h.activation(
                sqs[:], xt[:], AF.Square),
                wait=[t_ld] + ([t_rd2] if t_rd2 else []), sig=True)
            t_s1 = tk = pr.add("vector", lambda h, xt=xt, scr=scr:
                                h.reduce_sum(
                scr[:, 0:1], xt[:], axis=AX.X),
                wait=[t_ld] + ([t_scr_free[sl]] if t_scr_free[sl] else []),
                sig=True)
            t_rd2 = pr.add("vector", lambda h, scr=scr: h.reduce_sum(
                scr[:, 1:2], sqs[:], axis=AX.X), wait=[tq], sig=True)
            tk = pr.add("vector", lambda h, scr=scr: h.tensor_scalar(
                scr[:, 0:1], scr[:, 0:1], 1.0 / D, None, ALU.mult),
                wait=[tk, t_rd2], sig=True)
            tk = pr.add("vector", lambda h, scr=scr: h.tensor_scalar(
                scr[:, 1:2], scr[:, 1:2], 1.0 / D, None, ALU.mult),
                wait=[tk], sig=True)
            tk = pr.add("vector", lambda h, scr=scr: h.tensor_tensor(
                scr[:, 2:3], scr[:, 0:1], scr[:, 0:1], ALU.mult),
                wait=[tk], sig=True)
            tk = pr.add("vector", lambda h, scr=scr: h.tensor_tensor(
                scr[:, 3:4], scr[:, 1:2], scr[:, 2:3], ALU.subtract),
                wait=[tk], sig=True)
            tv = pr.add("vector", lambda h, scr=scr: h.tensor_scalar(
                scr[:, 4:5], scr[:, 3:4], 1e-5, None, ALU.add),
                wait=[tk], sig=True)
            ts_ = pr.add("scalar", lambda h, scr=scr: h.activation(
                scr[:, 4:5], scr[:, 4:5], AF.Sqrt), wait=[tv], sig=True)
            tk = pr.add("vector", lambda h, scr=scr: h.reciprocal(
                scr[:, 5:6], scr[:, 4:5]), wait=[ts_], sig=True)
            tn = pr.add("vector", lambda h, scr=scr: h.tensor_scalar(
                scr[:, 6:7], scr[:, 0:1], -1.0, scr[:, 5:6],
                ALU.mult, ALU.mult), wait=[tk], sig=True)
            t_xn = pr.add("scalar", lambda h, xn_t=xn_t, xt=xt, scr=scr:
                          h.activation(
                xn_t[:], xt[:], AF.Identity, bias=scr[:, 6:7],
                scale=scr[:, 5:6]),
                wait=[tn] + (free_xn[sl] or []), sig=True)
            free_x[sl] = [t_xn, t_s1]
            if not own:
                t_scr_free[sl] = t_xn

            for ch in range(NCH_D):
                pr.dma("scalar",
                       cx.xn_fm[:, ch * TW + j * P: ch * TW + (j + 1) * P],
                       xn_t[:, ch * P:(ch + 1) * P],
                       tpx[sl], wait=[t_xn] if ch == 0 else [], transpose=True)
            free_xn[sl] = [(tpx[sl], tpx[sl].n)]

            if own:
                tk = pr.add("vector", lambda h, xn_t=xn_t, scr=scr:
                            h.reduce_max(
                    scr[:, 7:8], xn_t[:], axis=AX.X,
                    apply_absolute_value=True), wait=[t_xn], sig=True)
                free_xn[sl].append(tk)
                tk = pr.add("vector", lambda h, scr=scr: h.reciprocal(
                    scr[:, 8:9], scr[:, 7:8]), wait=[tk], sig=True)
                tsp = pr.add("vector", lambda h, scr=scr: h.tensor_scalar(
                    scr[:, 8:9], scr[:, 8:9], 127.0, None, ALU.mult),
                    wait=[tk], sig=True)
                t_yp = pr.add("scalar", lambda h, xn_t=xn_t, scr=scr:
                              h.activation(
                    ypt[:], xn_t[:], AF.Identity, bias=cx.c1285[:],
                    scale=scr[:, 8:9]),
                    wait=[tsp, cx.t_ini] + ([t_xq_prev] if t_xq_prev else []),
                    sig=True)
                free_xn[sl].append(t_yp)
                tk = pr.add("vector", lambda h, scr=scr: h.tensor_tensor(
                    scr[:, 9:10], scr[:, 5:6], scr[:, 5:6], ALU.mult),
                    wait=[tsp], sig=True)
                tb = pr.add("vector", lambda h, scr=scr: h.tensor_scalar(
                    scr[:, 9:10], scr[:, 3:4], scr[:, 9:10], 1e-8,
                    ALU.mult, ALU.add), wait=[tk], sig=True)
                tc = pr.add("scalar", lambda h, scr=scr: h.activation(
                    scr[:, 9:10], scr[:, 9:10], AF.Sqrt), wait=[tb], sig=True)
                t_scr_free[sl] = tc
                tk = pr.add("vector", lambda h, scr=scr: h.reciprocal(
                    scr[:, 10:11], scr[:, 9:10]), wait=[tc], sig=True)
                tsv = pr.add("vector", lambda h, scr=scr: h.tensor_scalar(
                    scr[:, 10:11], scr[:, 10:11], scr[:, 7:8],
                    1.0 / 127.0, ALU.mult, ALU.mult), wait=[tk], sig=True)
                pr.add("vector", lambda h: h.tensor_copy(zzt[:], ypt[:]),
                       wait=[t_yp])
                for cc in BITS:
                    pr.add("vector", lambda h, c_=cc: h.tensor_single_scalar(
                        mmt[:], zzt[:], c_, ALU.is_ge))
                    pr.add("vector", lambda h, c_=cc: h.scalar_tensor_tensor(
                        zzt[:], mmt[:], -c_, zzt[:], ALU.mult, ALU.add))
                pr.add("vector", lambda h: h.scalar_tensor_tensor(
                    zzt[:], zzt[:], -1.0, ypt[:], ALU.mult, ALU.add))
                tu1 = pr.add("vector", lambda h, scr=scr: h.tensor_scalar(
                    mmt[:], zzt[:], 128.0, scr[:, 10:11],
                    ALU.subtract, ALU.mult),
                    wait=[tsv], sig=True)
                t_xq = pr.add("vector", lambda h, xq_t=xq_t: h.tensor_copy(
                    xq_t[:], mmt[:]),
                    wait=[tu1] + ([free_xq[sl]] if free_xq[sl] else []),
                    sig=True)
                t_xq_prev = t_xq
                jo = j - OWN0
                for ch in range(NCH_D):
                    pr.dma("scalar",
                           cx.xq_fm[:, ch * T_OWN + jo * P:
                                    ch * T_OWN + (jo + 1) * P],
                           xq_t[:, ch * P:(ch + 1) * P],
                           tpq[sl], wait=[t_xq] if ch == 0 else [],
                           transpose=True)
                free_xq[sl] = (tpq[sl], tpq[sl].n)

        if cx.d_scr is not None:
            s_dscr = pr.dma_sem("s_dscr")
            pr.dma("sync", cx.d_scr, scr2[:],
                   s_dscr, wait=[pr.tok("vector"), pr.tok("scalar")])
            pr.barrier(dma_full=[s_dscr, tpx[0], tpx[1], tpq[0], tpq[1],
                                 ldx[0], ldx[1]])
        else:
            pr.barrier(dma_full=[tpx[0], tpx[1], tpq[0], tpq[1],
                                 ldx[0], ldx[1]])


def _emit_scan_instance(cx, st, g, n, hl):
    pr = cx.pr
    pre = n < NPRE
    sl = st.inst % 2
    st.inst += 1
    cT = st.c_t[:, sl * C:(sl + 1) * C]
    eQ = st.eq_t[:, sl * C:(sl + 1) * C]
    eK = st.ek_t[:, sl * C:(sl + 1) * C]
    qT = st.qt_t[:, sl * C:(sl + 1) * C]
    kT = st.kt_t[:, sl * C:(sl + 1) * C]
    qH = st.qh_t[:, sl * C:(sl + 1) * C]
    kH = st.kh_t[:, sl * C:(sl + 1) * C]
    kTT = st.khT_t[:, sl * P:(sl + 1) * P]
    aSB = st.asb_t[:, sl * C:(sl + 1) * C]
    lnsl = st.ln_g[:, hl * TW + n * C: hl * TW + (n + 1) * C]
    kssl = st.ks_g[:, hl * TW + n * C: hl * TW + (n + 1) * C]
    if n % 2 == 0:
        vtile = st.v_g[0:C, (n // 2) * GW + hl * P:(n // 2) * GW + (hl + 1) * P]
    else:
        vtile = st.v_odd[0:C,
                         (n // 2) * GW + hl * P:(n // 2) * GW + (hl + 1) * P]
    r0 = cT[:, C // 2 - 1:C // 2]
    cend = cT[:, C - 1:C]
    f3 = st.sc1[:, 4 * sl + 0:4 * sl + 1]
    nr0 = st.sc1[:, 4 * sl + 1:4 * sl + 2]
    f1 = st.sc1[:, 4 * sl + 2:4 * sl + 3]
    f2 = st.sc1[:, 4 * sl + 3:4 * sl + 4]
    Sf = st.S_f[:, hl * P:(hl + 1) * P]
    Sb = st.S_b[:, hl * P:(hl + 1) * P]

    tc_ = pr.add("vector", lambda h, cT=cT, lnsl=lnsl: h.tensor_tensor_scan(
        cT, cx.ones_sc[:], lnsl, 0.0, ALU.mult, ALU.add),
        wait=[st.t_ln, st.t_pd_v, st.t_vodd] + st.freePrep[sl], sig=True)
    tf3 = pr.add("scalar", lambda h, f3=f3, cend=cend: h.activation(
        f3, cend, AF.Exp, bias=0.0, scale=1.0), wait=[tc_], sig=True)
    if pre:
        teh = pr.add("scalar", lambda h, eK=eK, cT=cT, cend=cend: h.activation(
            eK, cT, AF.Exp, bias=cend, scale=-1.0), sig=True)
        tkh = pr.add("gpsimd", lambda h, kH=kH, kssl=kssl, eK=eK:
                     h.tensor_tensor(kH, kssl, eK, ALU.mult),
                     wait=[teh], sig=True)
        t_sce_last = teh
    else:
        tnr = pr.add("vector", lambda h, nr0=nr0, r0=r0: h.tensor_scalar(
            nr0, r0, -1.0, None, ALU.mult), wait=[tc_], sig=True)
        td1 = pr.add("vector", lambda h, f1=f1, cend=cend, r0=r0:
                     h.tensor_tensor(f1, cend, r0, ALU.subtract), sig=True)
        teq = pr.add("scalar", lambda h, eQ=eQ, cT=cT, nr0=nr0: h.activation(
            eQ, cT, AF.Exp, bias=nr0, scale=1.0), wait=[tnr], sig=True)
        tek = pr.add("scalar", lambda h, eK=eK, cT=cT, r0=r0: h.activation(
            eK, cT, AF.Exp, bias=r0, scale=-1.0), sig=True)
        tf2 = pr.add("scalar", lambda h, f2=f2, r0=r0: h.activation(
            f2, r0, AF.Exp, bias=0.0, scale=1.0), sig=True)
        tf1 = pr.add("scalar", lambda h, f1=f1: h.activation(
            f1, f1, AF.Exp, bias=0.0, scale=1.0), wait=[td1], sig=True)
        t_sce_last = tf1
        qsl = st.qs_g[:, hl * T_OWN + (n - NPRE) * C:
                      hl * T_OWN + (n - NPRE + 1) * C]
        tqt = pr.add("gpsimd", lambda h, qT=qT, qsl=qsl, eQ=eQ:
                     h.tensor_tensor(qT, qsl, eQ, ALU.mult),
                     wait=[teq], sig=True)
        tkt = pr.add("gpsimd", lambda h, kT=kT, kssl=kssl, eK=eK:
                     h.tensor_tensor(kT, kssl, eK, ALU.mult),
                     wait=[tek], sig=True)
        tqh = pr.add("gpsimd", lambda h, qH=qH, qT=qT, f2=f2: h.tensor_scalar(
            qH, qT, f2, None, ALU.mult), wait=[tf2], sig=True)
        tkh = pr.add("gpsimd", lambda h, kH=kH, kT=kT, f1=f1: h.tensor_scalar(
            kH, kT, f1, None, ALU.mult), wait=[tf1], sig=True)

    tpe1 = pr.add("tensor", lambda h, ps=st.psKT[sl], kH=kH: h.transpose(
        ps[:], kH, cx.ident[:]),
        wait=[tkh] + ([st.freeKT[sl]] if st.freeKT[sl] else []), sig=True)
    tkhT = pr.add("vector", lambda h, kTT=kTT, ps=st.psKT[sl]: h.tensor_copy(
        kTT, ps[:]), wait=[tpe1], sig=True)
    st.freeKT[sl] = tkhT

    if not pre:
        tmA = pr.add("tensor", lambda h, ps=st.psA[sl], kT=kT, qT=qT:
                     h.matmul(ps[:], kT, qT, start=True, stop=True),
                     wait=[tqt, tkt] + ([st.freeA[sl]] if st.freeA[sl] else []),
                     sig=True)
        tasb = pr.add("vector", lambda h, aSB=aSB, ps=st.psA[sl]:
                      h.scalar_tensor_tensor(
            aSB, ps[:], 1e30, cx.maskT[:], ALU.min, ALU.mult),
            wait=[tmA, cx.t_ini], sig=True)
        st.freeA[sl] = tasb
        pr.add("tensor", lambda h, ps=st.psO[sl], vtile=vtile, aSB=aSB:
               h.matmul(ps[:], vtile, aSB, start=True, stop=False),
               wait=[tasb] + ([st.freeO[sl]] if st.freeO[sl] else []))
        tm2 = pr.add("tensor", lambda h, ps=st.psO[sl], Sb=Sb, qH=qH:
                     h.matmul(ps[:], Sb, qH, start=False, stop=True),
                     wait=[tqh, st.tSb[hl]], sig=True)
        st.tO2[hl] = tm2
        hh = g * HG + hl
        st.freeO[sl] = pr.add("vector", lambda h, ps=st.psO[sl], hh=hh, n=n:
                              h.tensor_copy(
            cx.o_fm[:, hh * T_OWN + (n - NPRE) * C:
                    hh * T_OWN + (n - NPRE + 1) * C], ps[:]),
            wait=[tm2], sig=True)

    tmDS = pr.add("tensor", lambda h, ps=st.psDS[sl], kTT=kTT, vtile=vtile:
                  h.matmul(ps[:], kTT, vtile, start=True, stop=True),
                  wait=[tkhT] + ([st.freeDS[sl]] if st.freeDS[sl] else []),
                  sig=True)
    tup = pr.add("vector", lambda h, Sf=Sf, f3=f3, ps=st.psDS[sl]:
                 h.scalar_tensor_tensor(Sf, Sf, f3, ps[:], ALU.mult, ALU.add),
                 wait=[tmDS, tf3], sig=True)
    st.freeDS[sl] = tup
    w_sb = [tup] + ([st.tO2[hl]] if st.tO2[hl] else [])
    st.tSb[hl] = pr.add("gpsimd", lambda h, Sb=Sb, Sf=Sf: h.tensor_copy(Sb, Sf),
                        wait=w_sb, sig=True)
    st.freePrep[sl] = [tmDS, t_sce_last, tup, st.tSb[hl]]


def _emit_group(cx, st, g, t_gprev):
    nc, pr = cx.nc, cx.pr
    gcol = g * GW
    for ch in range(NCH_D):
        for (dst, src) in ((st.Wq_g, cx.Wq_d), (st.Wk_g, cx.Wk_d),
                           (st.Wi_g, cx.Wi_d)):
            pr.dma("gpsimd", dst[:, ch * GW:(ch + 1) * GW],
                   src[ch * P:(ch + 1) * P, gcol:gcol + GW],
                   st.ldw, wait=t_gprev if ch == 0 else [])
    t_w = (st.ldw, st.ldw.n)

    with ExitStack() as ps:
        def pt(name, shape):
            return ps.enter_context(nc.psum_tensor(f"{name}_{g}", shape, F32))
        psQ = [pt("psQ0", [P, 512]), pt("psQ1", [P, 512])]
        psK = [pt("psK0", [P, 512]), pt("psK1", [P, 512])]
        psV = [pt("psV0", [P, GW]), pt("psV1", [P, GW])]
        freeQ = [None, None]
        freeK = [None, None]
        freeV = [None, None]
        qi = 0
        for hl in range(HG):
            for tb in range(T_OWN // 512):
                sl = qi % 2
                qi += 1
                w0 = [t_w] + ([freeQ[sl]] if freeQ[sl] else [])
                for ch in range(NCH_D):
                    tmm = pr.add("tensor", lambda h, ps_=psQ[sl], ch=ch, hl=hl,
                                 tb=tb: h.matmul(
                        ps_[:],
                        st.Wq_g[:, ch * GW + hl * P: ch * GW + (hl + 1) * P],
                        cx.xn_fm[:, ch * TW + T_PRE + tb * 512:
                                 ch * TW + T_PRE + (tb + 1) * 512],
                        start=(ch == 0), stop=(ch == NCH_D - 1)),
                        wait=w0 if ch == 0 else [], sig=(ch == NCH_D - 1))
                freeQ[sl] = pr.add("scalar", lambda h, ps_=psQ[sl], hl=hl,
                                   tb=tb: h.activation(
                    st.qs_g[:, hl * T_OWN + tb * 512:
                            hl * T_OWN + (tb + 1) * 512],
                    ps_[:], AF.Relu, bias=0.0, scale=QSCALE),
                    wait=[tmm], sig=True)
        ki = 0
        for hl in range(HG):
            t_sg = {}
            for tb in range(TW // 512):
                sl = ki % 2
                ki += 1
                w0 = [t_w] + ([freeK[sl]] if freeK[sl] else [])
                for ch in range(NCH_D):
                    tmm = pr.add("tensor", lambda h, ps_=psK[sl], ch=ch, hl=hl,
                                 tb=tb: h.matmul(
                        ps_[:],
                        st.Wk_g[:, ch * GW + hl * P: ch * GW + (hl + 1) * P],
                        cx.xn_fm[:, ch * TW + tb * 512:
                                 ch * TW + (tb + 1) * 512],
                        start=(ch == 0), stop=(ch == NCH_D - 1)),
                        wait=w0 if ch == 0 else [], sig=(ch == NCH_D - 1))
                pr.add("scalar", lambda h, ps_=psK[sl], hl=hl, tb=tb:
                       h.activation(
                    st.ks_g[:, hl * TW + tb * 512: hl * TW + (tb + 1) * 512],
                    ps_[:], AF.Sigmoid, bias=0.0, scale=1.0), wait=[tmm])
                t2 = pr.add("scalar", lambda h, ps_=psK[sl], tb=tb:
                            h.activation(
                    st.sg_g[:, tb * 512:(tb + 1) * 512],
                    ps_[:], AF.Sigmoid, bias=0.0, scale=-1.0), sig=True)
                freeK[sl] = t2
                t_sg[tb] = t2
            for tb in range(TW // 512):
                pr.add("scalar", lambda h, hl=hl, tb=tb: h.activation(
                    st.ln_g[:, hl * TW + tb * 512: hl * TW + (tb + 1) * 512],
                    st.sg_g[:, tb * 512:(tb + 1) * 512],
                    AF.Ln, bias=0.0, scale=1.0), wait=[t_sg[tb]])
        st.t_ln = pr.tok("scalar")
        vi = 0
        for tt in range(NTILE):
            sl = vi % 2
            vi += 1
            w0 = [t_w] + ([freeV[sl]] if freeV[sl] else [])
            for ch in range(NCH_D):
                tmm = pr.add("tensor", lambda h, ps_=psV[sl], ch=ch, tt=tt:
                             h.matmul(
                    ps_[:],
                    cx.xn_fm[:, ch * TW + tt * P: ch * TW + (tt + 1) * P],
                    st.Wi_g[:, ch * GW:(ch + 1) * GW],
                    start=(ch == 0), stop=(ch == NCH_D - 1)),
                    wait=w0 if ch == 0 else [], sig=(ch == NCH_D - 1))
            tve = pr.add("vector", lambda h, ps_=psV[sl], tt=tt:
                         h.tensor_copy(
                st.v_g[:, tt * GW:(tt + 1) * GW], ps_[:]),
                wait=[tmm], sig=True)
            freeV[sl] = tve
            pr.dma("sync", st.v_odd[:, tt * GW:(tt + 1) * GW],
                   st.v_g[C:P, tt * GW:(tt + 1) * GW], st.ldvo, wait=[tve])
        st.t_pd_v = pr.tok("vector")
        st.t_vodd = (st.ldvo, st.ldvo.n)
        t_projdone = [st.t_ln, st.t_pd_v, pr.tok("tensor")]

    if g == 0 and cx.d_qs is not None:
        sdbg1 = pr.dma_sem("sdbg1")
        pr.dma("sync", cx.d_qs, st.qs_g[:], sdbg1, wait=t_projdone)
        pr.dma("sync", cx.d_ks, st.ks_g[:], sdbg1)
        pr.dma("sync", cx.d_ln, st.ln_g[:], sdbg1)
        pr.dma("sync", cx.d_v, st.v_g[:], sdbg1)

    if cx.scan_mode == "proj":
        return [pr.tok("vector"), pr.tok("tensor"), pr.tok("scalar"),
                pr.tok("gpsimd")]
    with ExitStack() as ps:
        def pt2(name, shape, dt):
            return ps.enter_context(nc.psum_tensor(f"{name}_{g}", shape, dt))
        st.psA = [pt2("psA0", [C, C], F32), pt2("psA1", [C, C], F32)]
        st.psO = [pt2("psO0", [P, C], F32), pt2("psO1", [P, C], F32)]
        st.psDS = [pt2("psDS0", [P, P], F32), pt2("psDS1", [P, P], F32)]
        st.psKT = [pt2("psKT0", [C, P], BF16), pt2("psKT1", [C, P], BF16)]
        pr.add("gpsimd", lambda h: h.memset(st.S_f[:], 0.0), wait=t_projdone)
        tSb0 = pr.add("gpsimd", lambda h: h.memset(st.S_b[:], 0.0), sig=True)
        st.tSb = [tSb0, tSb0]
        st.tO2 = [None, None]
        st.freeA = [None, None]
        st.freeO = [None, None]
        st.freeDS = [None, None]
        st.freeKT = [None, None]
        st.freePrep = [[], []]
        st.inst = 0
        nch = NPRE if cx.scan_mode == "pre" else NCHUNK
        for n in range(nch):
            for hl in range(HG):
                _emit_scan_instance(cx, st, g, n, hl)

    if cx.scan_mode == "pre":
        return [pr.tok("vector"), pr.tok("tensor"), pr.tok("scalar"),
                pr.tok("gpsimd")]
    t_scan_done = [pr.tok("vector"), pr.tok("tensor"), pr.tok("gpsimd")]
    with nc.psum_tensor(f"psS2_{g}", [P, NTILE_OWN], F32) as psS2:
        tsq = {}
        for hl in range(HG):
            hh = g * HG + hl
            tsq[hl] = pr.add("scalar", lambda h, hl=hl, hh=hh: h.activation(
                st.o2_g[:, hl * T_OWN:(hl + 1) * T_OWN],
                cx.o_fm[:, hh * T_OWN:(hh + 1) * T_OWN], AF.Square),
                wait=t_scan_done, sig=True)
        tms = None
        for tt in range(NTILE_OWN):
            for hl in range(HG):
                tms = pr.add("tensor", lambda h, hl=hl, tt=tt: h.matmul(
                    psS2[:, tt:tt + 1],
                    st.o2_g[:, hl * T_OWN + tt * P: hl * T_OWN + (tt + 1) * P],
                    cx.ones_bf[:],
                    start=(hl == 0), stop=(hl == HG - 1)),
                    wait=[tsq[hl]] if tt == 0 else [],
                    sig=(tt == NTILE_OWN - 1 and hl == HG - 1))
        pr.add("vector", lambda h: h.tensor_tensor(
            cx.s2t_sb[:], cx.s2t_sb[:], psS2[:], ALU.add), wait=[tms])
    return [pr.tok("vector"), pr.tok("tensor"), pr.tok("scalar"),
            pr.tok("gpsimd")]


def _emit_p1(cx):
    nc, pr = cx.nc, cx.pr
    with ExitStack() as ps:
        def sb(name, shape, dt):
            return ps.enter_context(nc.sbuf_tensor(name, shape, dt))
        st = SimpleNamespace()
        st.ldw = pr.dma_sem("ldw_g")
        st.Wq_g = sb("Wq_g", [P, NCH_D * GW], BF16)
        st.Wk_g = sb("Wk_g", [P, NCH_D * GW], BF16)
        st.Wi_g = sb("Wi_g", [P, NCH_D * GW], BF16)
        st.qs_g = sb("qs_g", [P, HG * T_OWN], BF16)
        st.ks_g = sb("ks_g", [P, HG * TW], BF16)
        st.ln_g = sb("ln_g", [P, HG * TW], BF16)
        st.v_g = sb("v_g", [P, NTILE * GW], BF16)
        st.v_odd = sb("v_odd", [C, NTILE * GW], BF16)
        st.ldvo = pr.dma_sem("ldvo")
        st.sg_g = sb("sg_g", [P, TW], F32)
        st.S_f = sb("S_f", [P, HG * P], F32)
        st.S_b = sb("S_b", [P, HG * P], BF16)
        st.o2_g = sb("o2_g", [P, HG * T_OWN], BF16)
        st.c_t = sb("c_t", [P, 2 * C], F32)
        st.eq_t = sb("eq_t", [P, 2 * C], F32)
        st.ek_t = sb("ek_t", [P, 2 * C], F32)
        st.qt_t = sb("qt_t", [P, 2 * C], BF16)
        st.kt_t = sb("kt_t", [P, 2 * C], BF16)
        st.qh_t = sb("qh_t", [P, 2 * C], BF16)
        st.kh_t = sb("kh_t", [P, 2 * C], BF16)
        st.khT_t = sb("khT_t", [C, 2 * P], BF16)
        st.asb_t = sb("asb_t", [C, 2 * C], BF16)
        st.sc1 = sb("sc1", [P, 8], F32)

        t_gprev = []
        for g in range(cx.ngrp):
            t_gprev = _emit_group(cx, st, g, t_gprev)
    pr.barrier()


def _emit_p2(cx):
    nc, pr = cx.nc, cx.pr
    pr.add("vector", lambda h: h.tensor_scalar(
        cx.rs_t[:], cx.s2t_sb[:], 1.0 / D, 1e-5, ALU.mult, ALU.add))
    t1 = pr.add("scalar", lambda h: h.activation(
        cx.rs_t[:], cx.rs_t[:], AF.Sqrt), wait=[pr.tok("vector")], sig=True)
    t_rs = pr.add("vector", lambda h: h.reciprocal(
        cx.rs_t[:], cx.rs_t[:]), wait=[t1], sig=True)
    t_ogw = None
    for ch in range(NH):
        t_ogw = pr.add("vector", lambda h, ch=ch: h.tensor_scalar(
            cx.o_fm[:, ch * T_OWN:(ch + 1) * T_OWN],
            cx.o_fm[:, ch * T_OWN:(ch + 1) * T_OWN],
            cx.gw_sb[:, ch:ch + 1], None, ALU.mult),
            wait=[(cx.ld_ini, cx.ld_ini.n)] if ch == 0 else [], sig=True)

    ldwo = [pr.dma_sem("ldwo0"), pr.dma_sem("ldwo1")]
    stsat = [pr.dma_sem("stsat0"), pr.dma_sem("stsat1")]
    with ExitStack() as ps:
        def sb(name, shape, dt):
            return ps.enter_context(nc.sbuf_tensor(name, shape, dt))
        Wo_s = [sb("Wo_s0", [P, NCH_D * 512], BF16),
                sb("Wo_s1", [P, NCH_D * 512], BF16)]
        sat_t = [sb("sat_t0", [P, 512], BF16), sb("sat_t1", [P, 512], BF16)]
        psW = [ps.enter_context(nc.psum_tensor("psW0", [P, 512], F32)),
               ps.enter_context(nc.psum_tensor("psW1", [P, 512], F32))]
        freeWs = [None, None]
        freePs = [None, None]
        freeSt = [None, None]
        wi = 0
        for blk in range(4):
            wsl = blk % 2
            for ch in range(NCH_D):
                pr.dma("gpsimd",
                       Wo_s[wsl][:, ch * 512:(ch + 1) * 512],
                       cx.Wo_d[ch * P:(ch + 1) * P, blk * 512:(blk + 1) * 512],
                       ldwo[wsl],
                       wait=([freeWs[wsl]] if freeWs[wsl] and ch == 0 else []))
            t_wo = (ldwo[wsl], ldwo[wsl].n)
            for tt in range(NTILE_OWN):
                sl = wi % 2
                wi += 1
                w0 = [t_wo, t_ogw] + ([freePs[sl]] if freePs[sl] else [])
                for ch in range(NCH_D):
                    tmm = pr.add("tensor", lambda h, ps_=psW[sl], ch=ch, tt=tt,
                                 wsl=wsl: h.matmul(
                        ps_[:],
                        cx.o_fm[:, ch * T_OWN + tt * P:
                                ch * T_OWN + (tt + 1) * P],
                        Wo_s[wsl][:, ch * 512:(ch + 1) * 512],
                        start=(ch == 0), stop=(ch == NCH_D - 1)),
                        wait=w0 if ch == 0 else [], sig=(ch == NCH_D - 1))
                tev = pr.add("vector", lambda h, ps_=psW[sl], sl=sl, tt=tt:
                             h.tensor_scalar(
                    sat_t[sl][:], ps_[:], cx.rs_t[:, tt:tt + 1], None,
                    ALU.mult),
                    wait=[tmm, t_rs] + ([freeSt[sl]] if freeSt[sl] else []),
                    sig=True)
                freePs[sl] = tev
                freeSt[sl] = pr.dma(
                    "sync",
                    cx.sat_d[tt * P:(tt + 1) * P, blk * 512:(blk + 1) * 512],
                    sat_t[sl][:], stsat[sl], wait=[tev])
            freeWs[wsl] = pr.tok("tensor")
        pr.barrier(dma_full=[stsat[0], stsat[1], ldwo[0], ldwo[1]])


def _emit_ffn_pass(cx, st, e):
    nc, pr = cx.nc, cx.pr
    with ExitStack() as ps:
        psG = [ps.enter_context(nc.psum_tensor(f"psG0_{e}", [P, 512], F32)),
               ps.enter_context(nc.psum_tensor(f"psG1_{e}", [P, 512], F32))]
        psU = [ps.enter_context(nc.psum_tensor(f"psU0_{e}", [P, 512], F32)),
               ps.enter_context(nc.psum_tensor(f"psU1_{e}", [P, 512], F32))]
        freeG = [None, None]
        freeU = [None, None]
        freeGt = [None, None]
        t_swd = {}
        for i in range(E8):
            hc = e * E8 + i
            isl = i % 2
            tldg = pr.dma("sync",
                          st.stgG[:].rearrange("p (c h) -> p c h", c=NCH_D),
                          cx.WgT_d[:, hc * P:(hc + 1) * P]
                          .rearrange("(c p) h -> p c h", p=P),
                          st.ldfg, wait=[st.t_sgn_g] if st.t_sgn_g else [])
            st.t_sgn_g = pr.add("scalar", lambda h, isl=isl: h.activation(
                st.sWg[isl][:], st.stgG[:], AF.Sign),
                wait=[tldg] + ([st.t_gmm[isl]] if st.t_gmm[isl] else []) +
                     st.t_prev_down,
                sig=True)
            t_sw_g = st.t_sgn_g
            tldu = pr.dma("sync",
                          st.stgU[:].rearrange("p (c h) -> p c h", c=NCH_D),
                          cx.WuT_d[:, hc * P:(hc + 1) * P]
                          .rearrange("(c p) h -> p c h", p=P),
                          st.ldfu, wait=[st.t_sgn_u] if st.t_sgn_u else [])
            st.t_sgn_u = pr.add("scalar", lambda h, isl=isl: h.activation(
                st.sWu[isl][:], st.stgU[:], AF.Sign),
                wait=[tldu] + ([st.t_umm[isl]] if st.t_umm[isl] else []) +
                     st.t_prev_down,
                sig=True)
            t_sw_u = st.t_sgn_u
            tldd = pr.dma("sync", st.stgD[:],
                          cx.WdT_d[hc * P:(hc + 1) * P, :], st.ldfd,
                          wait=[st.t_sgn_d] if st.t_sgn_d else [])
            st.t_sgn_d = pr.add("scalar", lambda h, i=i: h.activation(
                st.sWd8[:, i * D:(i + 1) * D], st.stgD[:], AF.Sign),
                wait=[tldd] + st.t_prev_down, sig=True)
            t_swd[i] = st.t_sgn_d

            for tb in range(T_OWN // 512):
                slg = (2 * i + tb) % 2
                wg0 = [t_sw_g] + ([freeG[slg]] if freeG[slg] else []) + \
                    st.t_prev_down
                for ch in range(NCH_D):
                    tmg = pr.add("tensor", lambda h, ps_=psG[slg], ch=ch,
                                 isl=isl, tb=tb: h.matmul(
                        ps_[:],
                        st.sWg[isl][:, ch * P:(ch + 1) * P],
                        cx.xq_fm[:, ch * T_OWN + tb * 512:
                                 ch * T_OWN + (tb + 1) * 512],
                        start=(ch == 0), stop=(ch == NCH_D - 1)),
                        wait=wg0 if ch == 0 else [], sig=(ch == NCH_D - 1))
                st.t_gmm[isl] = tmg
                wu0 = [t_sw_u] + ([freeU[slg]] if freeU[slg] else [])
                for ch in range(NCH_D):
                    tmu = pr.add("tensor", lambda h, ps_=psU[slg], ch=ch,
                                 isl=isl, tb=tb: h.matmul(
                        ps_[:],
                        st.sWu[isl][:, ch * P:(ch + 1) * P],
                        cx.xq_fm[:, ch * T_OWN + tb * 512:
                                 ch * T_OWN + (tb + 1) * 512],
                        start=(ch == 0), stop=(ch == NCH_D - 1)),
                        wait=wu0 if ch == 0 else [], sig=(ch == NCH_D - 1))
                st.t_umm[isl] = tmu
                tsi = pr.add("scalar", lambda h, ps_=psG[slg], slg=slg:
                             h.activation(st.gtmp[slg][:], ps_[:], AF.Silu),
                             wait=[tmg] +
                                  ([freeGt[slg]] if freeGt[slg] else []),
                             sig=True)
                freeG[slg] = tsi
                tp_ = pr.add("vector", lambda h, ps_=psU[slg], slg=slg, i=i,
                             tb=tb: h.tensor_tensor(
                    st.p8[:, i * T_OWN + tb * 512: i * T_OWN + (tb + 1) * 512],
                    ps_[:], st.gtmp[slg][:], ALU.mult),
                    wait=[tmu, tsi] + st.t_prev_down, sig=True)
                freeU[slg] = tp_
                freeGt[slg] = tp_
        t_up_done = [pr.tok("vector"), pr.tok("tensor")]

    with ExitStack() as ps:
        psD = [ps.enter_context(nc.psum_tensor(f"psD0_{e}", [P, 4 * 512], F32)),
               ps.enter_context(nc.psum_tensor(f"psD1_{e}", [P, 4 * 512], F32))]
        freeD = [None, None]
        for tt in range(NTILE_OWN):
            dsl = tt % 2
            tdm = None
            for i in range(E8):
                w0 = []
                if i == 0:
                    w0 = t_up_done + [t_swd[j] for j in range(E8)]
                    if freeD[dsl]:
                        w0.append(freeD[dsl])
                for blk in range(4):
                    tdm = pr.add("tensor", lambda h, ps_=psD[dsl], i=i, tt=tt,
                                 blk=blk: h.matmul(
                        ps_[:, blk * 512:(blk + 1) * 512],
                        st.p8[:, i * T_OWN + tt * P: i * T_OWN + (tt + 1) * P],
                        st.sWd8[:, i * D + blk * 512: i * D + (blk + 1) * 512],
                        start=(i == 0), stop=(i == E8 - 1)),
                        wait=w0 if (i == 0 and blk == 0) else [],
                        sig=(i == E8 - 1 and blk == 3))
            tev = None
            for blk in range(4):
                if e == 0:
                    tev = pr.add("vector", lambda h, ps_=psD[dsl], tt=tt,
                                 blk=blk: h.tensor_copy(
                        st.ffn_acc[:, tt * D + blk * 512:
                                   tt * D + (blk + 1) * 512],
                        ps_[:, blk * 512:(blk + 1) * 512]),
                        wait=[tdm] if blk == 0 else [], sig=(blk == 3))
                else:
                    tev = pr.add("vector", lambda h, ps_=psD[dsl], tt=tt,
                                 blk=blk: h.tensor_tensor(
                        st.ffn_acc[:, tt * D + blk * 512:
                                   tt * D + (blk + 1) * 512],
                        st.ffn_acc[:, tt * D + blk * 512:
                                   tt * D + (blk + 1) * 512],
                        ps_[:, blk * 512:(blk + 1) * 512], ALU.add),
                        wait=[tdm] if blk == 0 else [], sig=(blk == 3))
            freeD[dsl] = tev
        st.t_prev_down = [pr.tok("tensor"), pr.tok("vector")]


def _emit_p3_p4(cx):
    nc, pr = cx.nc, cx.pr
    with ExitStack() as ps:
        def sb(name, shape, dt):
            return ps.enter_context(nc.sbuf_tensor(name, shape, dt))
        st = SimpleNamespace()
        st.ldfg = pr.dma_sem("ldfg")
        st.ldfu = pr.dma_sem("ldfu")
        st.ldfd = pr.dma_sem("ldfd")
        st.ffn_acc = sb("ffn_acc", [P, NTILE_OWN * D], F32)
        with ExitStack() as ps3:
            def sb3(name, shape, dt):
                return ps3.enter_context(nc.sbuf_tensor(name, shape, dt))
            st.p8 = sb3("p8", [P, E8 * T_OWN], BF16)
            st.stgG = sb3("stgG", [P, NCH_D * P], F32)
            st.stgU = sb3("stgU", [P, NCH_D * P], F32)
            st.stgD = sb3("stgD", [P, D], F32)
            st.sWg = [sb3("sWg0", [P, NCH_D * P], BF16),
                      sb3("sWg1", [P, NCH_D * P], BF16)]
            st.sWu = [sb3("sWu0", [P, NCH_D * P], BF16),
                      sb3("sWu1", [P, NCH_D * P], BF16)]
            st.sWd8 = sb3("sWd8", [P, E8 * D], BF16)
            st.gtmp = [sb3("gtmp0", [P, 512], BF16),
                       sb3("gtmp1", [P, 512], BF16)]
            st.t_sgn_g = None
            st.t_sgn_u = None
            st.t_sgn_d = None
            st.t_gmm = [None, None]
            st.t_umm = [None, None]
            st.t_prev_down = []
            for e in range(NE8):
                _emit_ffn_pass(cx, st, e)

            pr.barrier(dma_full=[st.ldfg, st.ldfu, st.ldfd])
        if cx.d_acc is not None:
            s_dbg4 = pr.dma_sem("s_dbg4")
            pr.dma("sync", cx.d_acc, st.ffn_acc[:], s_dbg4)
            pr.barrier(dma_full=[s_dbg4])

        ldcx = [pr.dma_sem("ldcx0"), pr.dma_sem("ldcx1")]
        ldcs = [pr.dma_sem("ldcs0"), pr.dma_sem("ldcs1")]
        stout = [pr.dma_sem("stout0"), pr.dma_sem("stout1")]
        xc = [sb("xc0", [P, D], F32), sb("xc1", [P, D], F32)]
        scb = [sb("sb0", [P, D], BF16), sb("sb1", [P, D], BF16)]
        ga = [sb("ga0", [P, D], BF16), sb("ga1", [P, D], BF16)]
        ot = [sb("ot0", [P, D], F32), sb("ot1", [P, D], F32)]
        freeOut = [None, None]
        for tt in range(NTILE_OWN):
            sl = tt % 2
            w0 = [freeOut[sl]] if freeOut[sl] else []
            tlx = pr.dma("sync", xc[sl][:], cx.x_own[tt * P:(tt + 1) * P, :],
                         ldcx[sl], wait=w0)
            tls = pr.dma("scalar", scb[sl][:],
                         cx.sat_d[tt * P:(tt + 1) * P, :], ldcs[sl], wait=w0)
            tga = pr.add("scalar", lambda h, sl=sl, tt=tt: h.activation(
                ga[sl][:], st.ffn_acc[:, tt * D:(tt + 1) * D], AF.Silu),
                wait=w0, sig=True)
            pr.add("vector", lambda h, sl=sl: h.tensor_tensor(
                ot[sl][:], ga[sl][:], scb[sl][:], ALU.mult),
                wait=[tga, tls] + w0)
            tfin = pr.add("vector", lambda h, sl=sl: h.tensor_tensor(
                ot[sl][:], ot[sl][:], xc[sl][:], ALU.add),
                wait=[tlx], sig=True)
            freeOut[sl] = pr.dma("sync", cx.out_d[tt * P:(tt + 1) * P, :],
                                 ot[sl][:], stout[sl], wait=[tfin])
        pr.wait_only("sync", [(stout[0], stout[0].n),
                              (stout[1], stout[1].n)])
        pr.wait_only("scalar", [(stout[0], stout[0].n),
                                (stout[1], stout[1].n)])


def build_program(dbg=(), upto="p4", ngrp=NGRP, scan_mode="full"):
    nc = bass.Bass("TRN2", target_bir_lowering=False, debug=False)
    dbg = set(dbg)
    cx = SimpleNamespace()
    cx.nc = nc
    cx.ngrp = ngrp
    cx.scan_mode = scan_mode

    cx.x_pre = nc.dram_tensor("x_pre", [T_PRE, D], F32, kind="ExternalInput").ap()
    cx.x_own = nc.dram_tensor("x_own", [T_OWN, D], F32, kind="ExternalInput").ap()
    cx.Wq_d = nc.dram_tensor("Wq", [D, D], F32, kind="ExternalInput").ap()
    cx.Wk_d = nc.dram_tensor("Wk", [D, D], F32, kind="ExternalInput").ap()
    cx.Wi_d = nc.dram_tensor("Wi", [D, D], F32, kind="ExternalInput").ap()
    cx.Wo_d = nc.dram_tensor("Wo", [D, D], F32, kind="ExternalInput").ap()
    cx.gw_d = nc.dram_tensor("gw", [D], F32, kind="ExternalInput").ap()
    cx.WgT_d = nc.dram_tensor("WgT", [D, HID], F32, kind="ExternalInput").ap()
    cx.WuT_d = nc.dram_tensor("WuT", [D, HID], F32, kind="ExternalInput").ap()
    cx.WdT_d = nc.dram_tensor("WdT", [HID, D], F32, kind="ExternalInput").ap()
    cx.out_d = nc.dram_tensor("out", [T_OWN, D], F32, kind="ExternalOutput").ap()
    cx.sat_d = nc.dram_tensor("sattn_scr", [T_OWN, D], BF16, kind="Internal").ap()

    def dbg_t(name, shape, dtype):
        if name in dbg:
            return nc.dram_tensor("dbg_" + name, shape, dtype,
                                  kind="ExternalOutput").ap()
        return None

    cx.d_scr = dbg_t("scr", [P, 12 * NTILE], F32)
    cx.d_yp15 = dbg_t("yp15", [P, D], BF16)
    cx.d_xq2 = dbg_t("xq2", [P, NCH_D * T_OWN], BF16)
    cx.d_xn = dbg_t("xn_fm", [P, NCH_D * TW], BF16)
    cx.d_xq = dbg_t("xq_fm", [P, NCH_D * T_OWN], BF16)
    cx.d_o = dbg_t("o_fm", [P, NH * T_OWN], BF16)
    cx.d_s2 = dbg_t("s2t", [P, NTILE_OWN], F32)
    cx.d_sat = dbg_t("sattn", [T_OWN, D], BF16)
    cx.d_acc = dbg_t("ffn_acc", [P, NTILE_OWN * D], F32)
    cx.d_qs = dbg_t("qs0", [P, HG * T_OWN], BF16)
    cx.d_ks = dbg_t("ks0", [P, HG * TW], BF16)
    cx.d_ln = dbg_t("ln0", [P, HG * TW], BF16)
    cx.d_v = dbg_t("v0", [P, NTILE * GW], BF16)

    with ExitStack() as es:
        pr = Prog(nc, es)
        cx.pr = pr

        def psb(name, shape, dt):
            return es.enter_context(nc.sbuf_tensor(name, shape, dt))
        cx.xq_fm = psb("xq_fm", [P, NCH_D * T_OWN], BF16)
        cx.s2t_sb = psb("s2t_sb", [P, NTILE_OWN], F32)
        cx.rs_t = psb("rs_t", [P, NTILE_OWN], F32)
        cx.gw_sb = psb("gw_sb", [P, NCH_D], F32)
        cx.c1285 = psb("c1285", [P, 1], F32)
        cx.ones_sc = psb("ones_sc", [P, C], F32)
        cx.ones_bf = psb("ones_bf", [P, 1], BF16)
        cx.maskT = psb("maskT", [C, C], F32)
        cx.ident = psb("ident", [P, P], BF16)

        block = es.enter_context(nc.Block())

        cx.ld_ini = pr.dma_sem("ld_ini")
        pr.add("gpsimd", lambda h: h.memset(cx.c1285[:], 128.5))
        pr.add("gpsimd", lambda h: h.memset(cx.ones_sc[:], 1.0))
        pr.add("gpsimd", lambda h: h.memset(cx.ones_bf[:], 1.0))
        pr.add("gpsimd", lambda h: h.memset(cx.s2t_sb[:], 0.0))
        pr.add("gpsimd", lambda h: h.iota(
            cx.maskT[:], [[1, C]], channel_multiplier=-1,
            allow_small_or_imprecise_dtypes=True))
        pr.add("gpsimd", lambda h: h.tensor_single_scalar(
            cx.maskT[:], cx.maskT[:], 0.0, ALU.is_ge))
        pr.add("gpsimd", lambda h: h.iota(
            cx.ident[:], [[1, P]], channel_multiplier=-1,
            allow_small_or_imprecise_dtypes=True))
        cx.t_ini = pr.add("gpsimd", lambda h: h.tensor_single_scalar(
            cx.ident[:], cx.ident[:], 0.0, ALU.is_equal), sig=True)
        pr.dma("gpsimd", cx.gw_sb[:],
               cx.gw_d.rearrange("(c p) -> p c", p=P), cx.ld_ini,
               allow_slow=True)

        with ExitStack() as es_o:
            cx.o_fm = es_o.enter_context(
                nc.sbuf_tensor("o_fm", [P, NH * T_OWN], BF16, side="right"))
            with ExitStack() as es_xn:
                cx.xn_fm = es_xn.enter_context(
                    nc.sbuf_tensor("xn_fm", [P, NCH_D * TW], BF16,
                                   side="right"))
                _emit_p0(cx)
                if cx.d_xn is not None or cx.d_xq is not None:
                    s_dbg = pr.dma_sem("s_dbg")
                    if cx.d_xn is not None:
                        pr.dma("sync", cx.d_xn, cx.xn_fm[:], s_dbg)
                    if cx.d_xq is not None:
                        pr.dma("sync", cx.d_xq, cx.xq_fm[:], s_dbg)
                    pr.barrier(dma_full=[s_dbg])
                    if cx.d_xq2 is not None:
                        pr.barrier()
                        pr.barrier()
                        pr.barrier()
                        pr.dma("sync", cx.d_xq2, cx.xq_fm[:], s_dbg)
                        pr.barrier(dma_full=[s_dbg])
                if upto != "p0":
                    _emit_p1(cx)
            if upto not in ("p0", "p1"):
                if cx.d_o is not None:
                    s_dbg2 = pr.dma_sem("s_dbg2")
                    pr.dma("sync", cx.d_o, cx.o_fm[:], s_dbg2)
                    pr.dma("sync", cx.d_s2, cx.s2t_sb[:], s_dbg2)
                    pr.barrier(dma_full=[s_dbg2])
                _emit_p2(cx)
        if upto == "p1" and cx.d_o is not None:
            s_dbg2 = pr.dma_sem("s_dbg2")
            pr.dma("sync", cx.d_o, cx.o_fm[:], s_dbg2)
            pr.dma("sync", cx.d_s2, cx.s2t_sb[:], s_dbg2)
            pr.barrier(dma_full=[s_dbg2])
        if upto not in ("p0", "p1", "p2"):
            if cx.d_sat is not None:
                s_dbg3 = pr.dma_sem("s_dbg3")
                pr.dma("sync", cx.d_sat, cx.sat_d, s_dbg3)
                pr.barrier(dma_full=[s_dbg3])
            _emit_p3_p4(cx)

        pr.finalize(block)
    return nc


_NC_CACHE = {}


def _get_nc(dbg=()):
    key = tuple(sorted(dbg))
    if key not in _NC_CACHE:
        _NC_CACHE[key] = build_program(dbg)
    return _NC_CACHE[key]


def make_in_maps(inputs):
    x = np.ascontiguousarray(np.asarray(inputs["L0_attn_hidden_states"],
                                        dtype=np.float32))
    Wq = np.ascontiguousarray(np.asarray(inputs["L0_attn_Wq"], np.float32))
    Wk = np.ascontiguousarray(np.asarray(inputs["L0_attn_Wk"], np.float32))
    Wi = np.ascontiguousarray(np.asarray(inputs["L0_attn_Wi"], np.float32))
    Wo = np.ascontiguousarray(np.asarray(inputs["L0_attn_Wo"], np.float32))
    gw = np.ascontiguousarray(np.asarray(inputs["L0_attn_g_weight"], np.float32))
    WgT = np.ascontiguousarray(np.asarray(inputs["L0_ffn_W_g"], np.float32).T)
    WuT = np.ascontiguousarray(np.asarray(inputs["L0_ffn_W_u"], np.float32).T)
    WdT = np.ascontiguousarray(np.asarray(inputs["L0_ffn_W_d"], np.float32).T)
    zeros = np.zeros([T_PRE, D], np.float32)
    in_maps = []
    for c in range(8):
        b, half = c // 2, c % 2
        x_ownc = np.ascontiguousarray(x[b, half * T_OWN:(half + 1) * T_OWN])
        x_prec = zeros if half == 0 else np.ascontiguousarray(x[b, 0:T_PRE])
        in_maps.append({
            "x_pre": x_prec, "x_own": x_ownc,
            "Wq": Wq, "Wk": Wk, "Wi": Wi, "Wo": Wo, "gw": gw,
            "WgT": WgT, "WuT": WuT, "WdT": WdT,
        })
    return in_maps


def kernel(**inputs):
    nc = _get_nc()
    in_maps = make_in_maps(inputs)
    res = run_bass_kernel_spmd(nc, in_maps, core_ids=list(range(8)))
    out = np.empty([4, 2048, D], np.float32)
    for c in range(8):
        b, half = c // 2, c % 2
        out[b, half * T_OWN:(half + 1) * T_OWN] = res.results[c]["out"]
    return out
